# version 01
# baseline (speedup 1.0000x reference)
import numpy as np
import ml_dtypes
import contextlib
import concourse.bass as bass
import concourse.mybir as mybir
from concourse.bass_utils import run_bass_kernel_spmd

F32 = mybir.dt.float32
BF16 = mybir.dt.bfloat16
I32 = mybir.dt.int32
AF = mybir.ActivationFunctionType
ALU = mybir.AluOpType
AX = mybir.AxisListType
BF = ml_dtypes.bfloat16

D = 2048
T = 4096
NCORES = 8
NT = 8
H0 = 5632
H0C = 44
HI = 8192
EPS_LN = 1e-5
DK = 128
NHEAD = 16
NKV = 4
QSCALE = 1.0 / np.sqrt(128.0)


def core_rows(c):
    s, k = divmod(c, 4)
    return np.concatenate([np.arange(s * T + (4 * j + k) * 128,
                                     s * T + (4 * j + k) * 128 + 128) for j in range(8)])

def img_tokens(x2d, c):
    r = core_rows(c)
    w = x2d.shape[1]
    out = np.empty((128, NT * w), x2d.dtype)
    for t in range(NT):
        out[:, t * w:(t + 1) * w] = x2d[r[t * 128:(t + 1) * 128]]
    return np.ascontiguousarray(out)

def unimg_tokens(img, w):
    out = np.empty((NT * 128, w), img.dtype)
    for t in range(NT):
        out[t * 128:(t + 1) * 128] = img[:, t * w:(t + 1) * w]
    return out

def wimg(wT):
    K, N = wT.shape
    nk = (K + 127) // 128
    out = np.zeros((128, nk * N), wT.dtype)
    for a in range(nk):
        rows = wT[a * 128:(a + 1) * 128]
        out[:rows.shape[0], a * N:(a + 1) * N] = rows
    return np.ascontiguousarray(out)


class Sem:
    def __init__(self, es, nc, name):
        self.h = es.enter_context(nc.semaphore(name))
        self.n = 0
    def inc(self, instr, k=1):
        instr.then_inc(self.h, k)
        self.n += k
        return self.n
    def wait(self, eng, val=None):
        eng.wait_ge(self.h, self.n if val is None else val)

def emit_ln(v, sc, S, tile_fn, n_tiles, width, stats, mvall, negm, sd, rstd, eps_tile):
    nchunk = (width + 511) // 512
    for t in range(n_tiles):
        xv = tile_fn(t)
        s0 = t * nchunk * 6
        for ch in range(nchunk):
            ist = v.bn_stats(stats[:, s0 + 6 * ch:s0 + 6 * ch + 6], xv[:, ch * 512:min(width, (ch + 1) * 512)])
            S['v'].wait(v, S['v'].inc(ist))
        ia = v.bn_aggr(mvall[:, 2 * t:2 * t + 2], stats[:, s0:s0 + 6 * nchunk])
        S['v'].wait(v, S['v'].inc(ia))
    for t in range(n_tiles):
        i1 = v.tensor_scalar(negm[:, t:t + 1], mvall[:, 2 * t:2 * t + 1], -1.0, None, ALU.mult)
    val = S['v'].inc(i1)
    S['v'].wait(sc, val)
    for t in range(n_tiles):
        i2 = sc.activation(sd[:, t:t + 1], mvall[:, 2 * t + 1:2 * t + 2], AF.Sqrt,
                           bias=eps_tile[:, 0:1], scale=1.0)
    sval = S['s'].inc(i2)
    S['s'].wait(v, sval)
    i3 = v.reciprocal(rstd[:, 0:n_tiles], sd[:, 0:n_tiles])
    rv = S['v'].inc(i3)
    S['v'].wait(v, rv)
    return rv


def build_A(nc, dbg=False):
    x_in = nc.dram_tensor("x_img", [128, NT * D], F32, kind="ExternalInput").ap()
    wkvv_in = nc.dram_tensor("wkvv", [128, 16 * 256], F32, kind="ExternalInput").ap()
    bkvv_in = nc.dram_tensor("bkvv", [1, 256], F32, kind="ExternalInput").ap()
    identb_in = nc.dram_tensor("identb", [128, 128], BF16, kind="ExternalInput").ap()
    ident32_in = nc.dram_tensor("ident32", [128, 128], F32, kind="ExternalInput").ap()
    w1s_in = nc.dram_tensor("w1s", [128, 2 * H0], F32, kind="ExternalInput").ap()
    w2s_in = nc.dram_tensor("w2s", [128, 2 * H0], F32, kind="ExternalInput").ap()
    w3s_in = nc.dram_tensor("w3s", [128, 6 * D], F32, kind="ExternalInput").ap()
    x1_out = nc.dram_tensor("x1", [128, NT * D], F32, kind="ExternalOutput").ap()
    if dbg:
        dbg_xln = nc.dram_tensor("dbg_xln", [128, NT * D], F32, kind="ExternalOutput").ap()
        dbg_xlnT = nc.dram_tensor("dbg_xlnT", [128, 16 * 1024], F32, kind="ExternalOutput").ap()
        dbg_vfm = nc.dram_tensor("dbg_vfm", [128, 2048], F32, kind="ExternalOutput").ap()
        dbg_vT = nc.dram_tensor("dbg_vT", [128, NT * 256], F32, kind="ExternalOutput").ap()
        dbg_mv = nc.dram_tensor("dbg_mv", [128, 16], F32, kind="ExternalOutput").ap()
        dbg_negm = nc.dram_tensor("dbg_negm", [128, 8], F32, kind="ExternalOutput").ap()
        dbg_rstd = nc.dram_tensor("dbg_rstd", [128, 8], F32, kind="ExternalOutput").ap()
    pabs_out = nc.dram_tensor("pabs", [1, 4], F32, kind="ExternalOutput").ap()

    with contextlib.ExitStack() as es:
        sb = lambda n_, s_, d_: es.enter_context(nc.sbuf_tensor(n_, s_, d_))
        ps = lambda n_, s_, d_: es.enter_context(nc.psum_tensor(n_, s_, d_))
        xs = sb("xs", [128, NT * D], F32)
        xln = sb("xln", [128, NT * D], BF16)
        xlnT = sb("xlnT", [128, 16 * 1024], BF16)
        wkv32 = sb("wkv32", [128, 4096], F32)
        wkvb = sb("wkvb", [128, 4096], BF16)
        bkv32 = sb("bkv32", [1, 256], F32)
        bkvb = sb("bkvb", [1, 256], BF16)
        sidb = sb("sidb", [128, 128], BF16)
        sid32 = sb("sid32", [128, 128], F32)
        ones_row = sb("ones_row", [1, 512], BF16)
        ones_c32 = sb("ones_c32", [128, 1], F32)
        eps_tile = sb("eps_tile", [128, 1], F32)
        stats = sb("stats", [128, NT * 24], F32)
        mvall = sb("mvall", [128, 16], F32)
        negm = sb("negm", [128, 8], F32)
        sd = sb("sd", [128, 8], F32)
        rstd = sb("rstd", [128, 8], F32)
        vfm = sb("vfm", [128, 2048], F32)
        vT = sb("vT", [128, NT * 256], F32)
        lnv = sb("lnv", [128, 2 * 256], F32)
        ws = sb("ws", [128, H0 + 512], F32)
        absacc = sb("absacc", [128, 8], F32)
        mcol = sb("mcol", [128, 3], F32)
        pabs_sb = sb("pabs_sb", [1, 4], F32)
        dbg1 = sb("dbg1", [128, 32], F32)
        ptrb = [ps("ptrb0", [128, 512], BF16), ps("ptrb1", [128, 512], BF16)]
        pvb = [ps("pvb0", [128, 512], F32), ps("pvb1", [128, 512], F32)]
        ptr32 = [ps("ptr32_0", [128, 512], F32), ps("ptr32_1", [128, 512], F32)]

        S = {k: Sem(es, nc, "sem_" + k) for k in ['ld', 'ld2', 'v', 's', 'te', 'st', 'g']}
        block = es.enter_context(nc.Block())

        @block.vector
        def _(v):
            sy, sc, te, g = nc.sync, nc.scalar, nc.tensor, nc.gpsimd
            xlnT3 = xlnT[:].rearrange("p (a c) -> p a c", c=1024)
            vT3 = vT[:].rearrange("p (t c) -> p t c", c=256)
            S['ld'].inc(sy.dma_start(out=xs[:], in_=x_in[:, :]), 16)
            S['ld'].inc(sy.dma_start(out=wkv32[:], in_=wkvv_in[:, :]), 16)
            S['ld'].inc(sy.dma_start(out=bkv32[:], in_=bkvv_in[:, :]), 16)
            S['ld'].inc(sy.dma_start(out=sidb[:], in_=identb_in[:, :]), 16)
            S['ld'].inc(sy.dma_start(out=sid32[:], in_=ident32_in[:, :]), 16)
            ld_all = S['ld'].n
            v.memset(ones_row[:], 1.0)
            v.memset(ones_c32[:], 1.0)
            i_eps = v.memset(eps_tile[:], EPS_LN)
            S['v'].inc(i_eps)
            S['ld'].wait(sc, ld_all)
            sc.activation(wkvb[:], wkv32[:], AF.Copy, bias=0.0, scale=1.0)
            cast_done = S['s'].inc(sc.activation(bkvb[:], bkv32[:], AF.Copy, bias=0.0, scale=1.0))
            S['ld'].wait(v, 16)
            emit_ln(v, sc, S, lambda t: xs[:, t * D:(t + 1) * D], NT, D,
                    stats, mvall, negm, sd, rstd, eps_tile)
            if dbg:
                v.tensor_copy(dbg1[:, 0:8], negm[:, 0:8])
                v.tensor_copy(dbg1[:, 8:16], rstd[:, 0:8])
                v.tensor_copy(dbg1[:, 16:32], mvall[:, 0:16])
            for t in range(NT):
                i = v.tensor_scalar(xln[:, t * D:(t + 1) * D], xs[:, t * D:(t + 1) * D],
                                    negm[:, t:t + 1], rstd[:, t:t + 1], ALU.add, ALU.mult)
            xln_done = S['v'].inc(i)
            S['ld'].wait(te, ld_all)
            S['v'].wait(te, xln_done)
            evac_vals = []
            gi = 0
            for t in range(NT):
                for q in range(4):
                    bank = ptrb[gi % 2]
                    if gi >= 2:
                        S['v'].wait(te, evac_vals[gi - 2])
                    for i in range(4):
                        dc = q * 4 + i
                        ins = te.transpose(bank[:, i * 128:(i + 1) * 128],
                                           xln[:, t * D + dc * 128:t * D + (dc + 1) * 128],
                                           sidb[:])
                    tv = S['te'].inc(ins)
                    S['te'].wait(v, tv)
                    dst = xlnT3[:, 4 * q:4 * q + 4, t * 128:(t + 1) * 128]
                    evac_vals.append(S['v'].inc(v.tensor_copy(dst, bank[:])))
                    gi += 1
            xlnT_done = S['v'].n
            S['s'].wait(te, cast_done)
            S['v'].wait(te, xlnT_done)
            vp_vals = []
            for idx, (vc, tb) in enumerate([(a, b) for a in range(2) for b in range(2)]):
                bank = pvb[idx % 2]
                if idx >= 2:
                    S['v'].wait(te, vp_vals[idx - 2])
                for dc in range(16):
                    te.matmul(bank[:], wkvb[:, dc * 256 + vc * 128:dc * 256 + (vc + 1) * 128],
                              xlnT3[:, dc, tb * 512:(tb + 1) * 512],
                              start=(dc == 0), stop=False)
                ins = te.matmul(bank[:], bkvb[:, vc * 128:(vc + 1) * 128], ones_row[:],
                                start=False, stop=True)
                tv = S['te'].inc(ins)
                S['te'].wait(v, tv)
                vp_vals.append(S['v'].inc(v.tensor_copy(
                    vfm[:, vc * 1024 + tb * 512:vc * 1024 + (tb + 1) * 512], bank[:])))
            S['v'].wait(te)
            vt_vals = []
            for grp in range(4):
                vc, th = divmod(grp, 2)
                bank = ptr32[grp % 2]
                if grp >= 2:
                    S['v'].wait(te, vt_vals[grp - 2])
                for i in range(4):
                    t = th * 4 + i
                    ins = te.transpose(bank[:, i * 128:(i + 1) * 128],
                                       vfm[:, vc * 1024 + t * 128:vc * 1024 + (t + 1) * 128],
                                       sid32[:])
                tv = S['te'].inc(ins)
                S['te'].wait(v, tv)
                dst = vT3[:, th * 4:th * 4 + 4, vc * 128:(vc + 1) * 128]
                vt_vals.append(S['v'].inc(v.tensor_copy(dst, bank[:])))
            emit_ln(v, sc, S, lambda t: vT[:, t * 256:(t + 1) * 256], NT, 256,
                    stats, mvall, negm, sd, rstd, eps_tile)
            for t in range(NT):
                lb = (t % 2) * 256
                v.tensor_scalar(lnv[:, lb:lb + 256], vT[:, t * 256:(t + 1) * 256],
                                negm[:, t:t + 1], rstd[:, t:t + 1], ALU.add, ALU.mult)
                for gq in range(4):
                    for h in range(8):
                        c0 = t * D + gq * 512 + h * 64
                        i = v.tensor_tensor(xs[:, c0:c0 + 64], xs[:, c0:c0 + 64],
                                            lnv[:, lb + gq * 64:lb + gq * 64 + 64], op=ALU.add)
            x_done = S['v'].inc(i)
            S['v'].wait(g, x_done)
            S['st'].inc(g.dma_start(out=x1_out[:, :], in_=xs[:]), 16)
            pieces = [(w1s_in, 0, H0, 0), (w1s_in, H0, H0, 1),
                      (w2s_in, 0, H0, 2), (w2s_in, H0, H0, 3),
                      (w3s_in, 0, 3 * D, 4), (w3s_in, 3 * D, 3 * D, 5)]
            red_done = []
            for pi, (srcap, c0, w, col) in enumerate(pieces):
                if pi > 0:
                    S['v'].wait(sy, red_done[pi - 1])
                lv = S['ld2'].inc(sy.dma_start(out=ws[:, 0:w], in_=srcap[:, c0:c0 + w]), 16)
                S['ld2'].wait(v, lv)
                ins = v.tensor_reduce(absacc[:, col:col + 1], ws[:, 0:w],
                                      axis=AX.X, op=ALU.add, apply_absolute_value=True)
                red_done.append(S['v'].inc(ins))
                S['v'].wait(v, red_done[-1])
            for m in range(3):
                ins = v.tensor_tensor(mcol[:, m:m + 1], absacc[:, 2 * m:2 * m + 1],
                                      absacc[:, 2 * m + 1:2 * m + 2], op=ALU.add)
            S['v'].inc(ins)
            if dbg:
                S['st'].inc(g.dma_start(out=dbg_xln[:, :], in_=xln[:]), 16)
                S['st'].inc(g.dma_start(out=dbg_xlnT[:, :], in_=xlnT[:]), 16)
                S['st'].inc(g.dma_start(out=dbg_vfm[:, :], in_=vfm[:]), 16)
                S['st'].inc(g.dma_start(out=dbg_vT[:, :], in_=vT[:]), 16)
                S['st'].inc(g.dma_start(out=dbg_mv[:, :], in_=dbg1[:, 16:32]), 16)
                S['st'].inc(g.dma_start(out=dbg_negm[:, :], in_=dbg1[:, 0:8]), 16)
                S['st'].inc(g.dma_start(out=dbg_rstd[:, :], in_=dbg1[:, 8:16]), 16)
            mz = S['v'].inc(v.memset(pabs_sb[:], 0.0))
            mc_done = S['v'].n
            S['v'].wait(g, mc_done)
            gv = S['g'].inc(g.tensor_reduce(pabs_sb[0:1, 0:3], mcol[:, 0:3],
                                            axis=AX.C, op=ALU.add))
            S['st'].inc(g.dma_start(out=pabs_out[:, :], in_=pabs_sb[:]), 16)
            S['st'].wait(g)
    return nc


def build_B(nc):
    w1s_in = nc.dram_tensor("w1s", [128, 2 * H0], F32, kind="ExternalInput").ap()
    w2s_in = nc.dram_tensor("w2s", [128, 2 * H0], F32, kind="ExternalInput").ap()
    w3s_in = nc.dram_tensor("w3s", [128, 6 * D], F32, kind="ExternalInput").ap()
    pall_in = nc.dram_tensor("pall", [1, 24], F32, kind="ExternalInput").ap()
    q1_out = nc.dram_tensor("q1", [128, 2 * H0], BF16, kind="ExternalOutput").ap()
    q2_out = nc.dram_tensor("q2", [128, 2 * H0], BF16, kind="ExternalOutput").ap()
    q3_out = nc.dram_tensor("q3", [128, 6 * D], BF16, kind="ExternalOutput").ap()
    gam_d = nc.dram_tensor("gam_d", [1, 4], F32, kind="Internal").ap()
    dbg_r = nc.dram_tensor("dbg_r", [128, 3], F32, kind="ExternalOutput").ap()

    with contextlib.ExitStack() as es:
        sb = lambda n_, s_, d_: es.enter_context(nc.sbuf_tensor(n_, s_, d_))
        ws = sb("wsB", [128, 6 * D], F32)
        wi = sb("wiB", [128, 6 * D], F32)
        wq = sb("wqB", [128, 6 * D], BF16)
        pall = sb("pallB", [1, 24], F32)
        gam = sb("gamB", [1, 4], F32)
        rcol = sb("rcolB", [128, 3], F32)
        S = {k: Sem(es, nc, "semB_" + k) for k in ['ld', 'v', 'g', 'st']}
        block = es.enter_context(nc.Block())

        @block.vector
        def _(v):
            sy, sc, te, g = nc.sync, nc.scalar, nc.tensor, nc.gpsimd
            lv0 = S['ld'].inc(sy.dma_start(out=pall[:], in_=pall_in[:, :]), 16)
            S['ld'].wait(v, lv0)
            counts = [H0 * D, H0 * D, H0 * D]
            for m in range(3):
                ir = v.tensor_reduce(gam[:, m:m + 1], pall[:, 8 * m:8 * m + 8], axis=AX.X, op=ALU.add)
                S['v'].wait(v, S['v'].inc(ir))
            i = v.tensor_scalar(gam[:, 0:3], gam[:, 0:3], 1.0 / (H0 * D), 1e-8, ALU.mult, ALU.add)
            iv = S['v'].inc(i)
            S['v'].wait(v, iv)
            i = v.reciprocal(gam[:, 0:3], gam[:, 0:3])
            iv = S['v'].inc(i)
            S['v'].wait(g, iv)
            gv = S['g'].inc(g.dma_start(out=gam_d[0:1, 0:3], in_=gam[:, 0:3]), 16)
            S['g'].wait(g, gv)
            gv = S['g'].inc(g.dma_start(out=rcol[:, 0:3],
                                        in_=gam_d[0:1, 0:3].squeeze(0).partition_broadcast(128)), 16)
            S['g'].wait(v, gv)
            S['st'].inc(g.dma_start(out=dbg_r[:, :], in_=rcol[:, 0:3]), 16)
            for m, (src, dst, ncolw) in enumerate([(w1s_in, q1_out, 2 * H0),
                                                   (w2s_in, q2_out, 2 * H0),
                                                   (w3s_in, q3_out, 6 * D)]):
                lv = S['ld'].inc(sy.dma_start(out=ws[:, 0:ncolw], in_=src[:, :]), 16)
                S['ld'].wait(v, lv)
                v.tensor_scalar(wi[:, 0:ncolw], ws[:, 0:ncolw], rcol[:, m:m + 1], 0.5,
                                ALU.mult, ALU.is_ge)
                v.tensor_scalar(ws[:, 0:ncolw], ws[:, 0:ncolw], rcol[:, m:m + 1], -0.5,
                                ALU.mult, ALU.is_le)
                i = v.tensor_tensor(wq[:, 0:ncolw], wi[:, 0:ncolw], ws[:, 0:ncolw],
                                    op=ALU.subtract)
                qv = S['v'].inc(i)
                S['v'].wait(g, qv)
                S['st'].inc(g.dma_start(out=dst[:, :], in_=wq[:, 0:ncolw]), 16)
                S['st'].wait(sy)
            S['st'].wait(g)
    return nc


def build_C(nc):
    x_in = nc.dram_tensor("x_img", [128, NT * D], F32, kind="ExternalInput").ap()
    w1q_in = nc.dram_tensor("w1q", [128, H0C * 2048], BF16, kind="ExternalInput").ap()
    w2q_in = nc.dram_tensor("w2q", [128, H0C * 2048], BF16, kind="ExternalInput").ap()
    w3q_in = nc.dram_tensor("w3q", [H0, D], BF16, kind="ExternalInput").ap()
    identb_in = nc.dram_tensor("identb", [128, 128], BF16, kind="ExternalInput").ap()
    x2_out = nc.dram_tensor("x2", [128, NT * D], F32, kind="ExternalOutput").ap()

    with contextlib.ExitStack() as es:
        sb = lambda n_, s_, d_: es.enter_context(nc.sbuf_tensor(n_, s_, d_))
        xt = sb("xtC", [128, 2 * D], F32)
        xlnt = sb("xlntC", [128, 2 * D], BF16)
        xlnT = sb("xlnTC", [128, 16 * 1024], BF16)
        hidT = sb("hidTC", [128, H0C * 1024], BF16)
        sidb = sb("sidbC", [128, 128], BF16)
        eps_tile = sb("epsC", [128, 1], F32)
        stats = sb("statsC", [128, NT * 24], F32)
        mvall = sb("mvallC", [128, 16], F32)
        negm = sb("negmC", [128, 8], F32)
        sd = sb("sdC", [128, 8], F32)
        rstd = sb("rstdC", [128, 8], F32)
        w1t = sb("w1tC", [128, 2 * 2048], BF16)
        w2t = sb("w2tC", [128, 2 * 2048], BF16)
        w3t = sb("w3tC", [128, 2 * 2048], BF16)
        sg = sb("sgC", [128, 2 * 512], F32)
        xout = sb("xoutC", [128, 2 * D], F32)
        xin2 = sb("xin2C", [128, 2 * D], F32)
        S = {k: Sem(es, nc, "semC_" + k) for k in
             ['ldx', 'ldw', 'ldw3', 'ldxr', 'v', 's', 'te', 'st']}
        es1 = contextlib.ExitStack()
        ptrb = [es1.enter_context(nc.psum_tensor("ptrbC0", [128, 512], BF16)),
                es1.enter_context(nc.psum_tensor("ptrbC1", [128, 512], BF16))]
        pf = [es1.enter_context(nc.psum_tensor(f"pfC{i}", [128, 512], F32)) for i in range(4)]
        block = es.enter_context(nc.Block())

        @block.vector
        def _(v):
            sy, sc, te, g = nc.sync, nc.scalar, nc.tensor, nc.gpsimd
            xlnT3 = xlnT[:].rearrange("p (a c) -> p a c", c=1024)
            v.memset(eps_tile[:], EPS_LN)
            S['ldx'].inc(sy.dma_start(out=sidb[:], in_=identb_in[:, :]), 16)
            apply_done = []
            evac_done = []
            tg_done = []
            for t in range(2):
                S['ldx'].inc(sy.dma_start(out=xt[:, (t % 2) * D:(t % 2 + 1) * D],
                                          in_=x_in[:, t * D:(t + 1) * D]), 16)
            for t in range(NT):
                tb = (t % 2) * D
                S['ldx'].wait(v, 16 * (t + 2))
                emit_ln(v, sc, S, lambda _t: xt[:, tb:tb + D], 1, D,
                        stats[:, t * 24:], mvall[:, 2 * t:], negm[:, t:], sd[:, t:],
                        rstd[:, t:], eps_tile)
                ia = v.tensor_scalar(xlnt[:, tb:tb + D], xt[:, tb:tb + D],
                                     negm[:, t:t + 1], rstd[:, t:t + 1], ALU.add, ALU.mult)
                apply_done.append(S['v'].inc(ia))
                if t + 2 < NT:
                    S['v'].wait(sy, apply_done[t])
                    S['ldx'].inc(sy.dma_start(out=xt[:, ((t + 2) % 2) * D:((t + 2) % 2 + 1) * D],
                                              in_=x_in[:, (t + 2) * D:(t + 3) * D]), 16)
                for q in range(4):
                    gi = t * 4 + q
                    bank = ptrb[gi % 2]
                    S['v'].wait(te, apply_done[-1])
                    if gi >= 2:
                        S['v'].wait(te, evac_done[gi - 2])
                    if gi == 0:
                        S['ldx'].wait(te, 16)
                    for i in range(4):
                        dc = q * 4 + i
                        ins = te.transpose(bank[:, i * 128:(i + 1) * 128],
                                           xlnt[:, tb + dc * 128:tb + (dc + 1) * 128], sidb[:])
                    tg_done.append(S['te'].inc(ins))
                    S['te'].wait(v, tg_done[-1])
                    dst = xlnT3[:, 4 * q:4 * q + 4, t * 128:(t + 1) * 128]
                    evac_done.append(S['v'].inc(v.tensor_copy(dst, bank[:])))
            xlnT_done = S['v'].n
            S['v'].wait(te, xlnT_done)
            grp_sg = []
            grp_hid = []
            grp_mm = []
            for gi2 in range(2 * H0C):
                hc, tbk = divmod(gi2, 2)
                if tbk == 0:
                    if hc >= 2:
                        S['te'].wait(sy, grp_mm[(hc - 2) * 2 + 1][1])
                    S['ldw'].inc(sy.dma_start(out=w1t[:, (hc % 2) * 2048:(hc % 2 + 1) * 2048],
                                              in_=w1q_in[:, hc * 2048:(hc + 1) * 2048]), 16)
                    S['ldw'].inc(sy.dma_start(out=w2t[:, (hc % 2) * 2048:(hc % 2 + 1) * 2048],
                                              in_=w2q_in[:, hc * 2048:(hc + 1) * 2048]), 16)
                b1, b2 = pf[(gi2 % 2) * 2], pf[(gi2 % 2) * 2 + 1]
                wof = (hc % 2) * 2048
                S['ldw'].wait(te, 32 * (hc + 1))
                if gi2 >= 2:
                    S['v'].wait(te, grp_hid[gi2 - 2])
                for dc in range(16):
                    i1 = te.matmul(b1[:], w1t[:, wof + dc * 128:wof + (dc + 1) * 128],
                                   xlnT3[:, dc, tbk * 512:(tbk + 1) * 512],
                                   start=(dc == 0), stop=(dc == 15))
                p1v = S['te'].inc(i1)
                for dc in range(16):
                    i2 = te.matmul(b2[:], w2t[:, wof + dc * 128:wof + (dc + 1) * 128],
                                   xlnT3[:, dc, tbk * 512:(tbk + 1) * 512],
                                   start=(dc == 0), stop=(dc == 15))
                p2v = S['te'].inc(i2)
                grp_mm.append((p1v, p2v))
                S['te'].wait(sc, p1v)
                if gi2 >= 2:
                    S['v'].wait(sc, grp_hid[gi2 - 2])
                isg = sc.activation(sg[:, (gi2 % 2) * 512:(gi2 % 2 + 1) * 512], b1[:],
                                    AF.Sigmoid, bias=0.0, scale=1.0)
                grp_sg.append(S['s'].inc(isg))
                S['te'].wait(v, p2v)
                S['s'].wait(v, grp_sg[-1])
                ih = v.scalar_tensor_tensor(
                    hidT[:, hc * 1024 + tbk * 512:hc * 1024 + (tbk + 1) * 512],
                    b2[:], 1.0, sg[:, (gi2 % 2) * 512:(gi2 % 2 + 1) * 512],
                    ALU.mult, ALU.mult)
                grp_hid.append(S['v'].inc(ih))
            hid_done = S['v'].n
            es1.close()
            px = [es.enter_context(nc.psum_tensor(f"pxC{i}", [128, 512], F32)) for i in range(8)]
            S['v'].wait(te, hid_done)
            pair_apply = []
            w3_cons = []
            for pair in range(4):
                if pair >= 1:
                    S['v'].wait(sy, pair_apply[pair - 1])
                for i in range(2):
                    t = pair * 2 + i
                    S['ldxr'].inc(sy.dma_start(
                        out=xin2[:, i * D:(i + 1) * D],
                        in_=x_in[:, t * D:(t + 1) * D]), 16)
                for hc in range(H0C):
                    di = pair * H0C + hc
                    if di >= 2:
                        S['te'].wait(sy, w3_cons[di - 2])
                    S['ldw3'].inc(sy.dma_start(out=w3t[:, (di % 2) * 2048:(di % 2 + 1) * 2048],
                                               in_=w3q_in[hc * 128:(hc + 1) * 128, :]), 16)
                    S['ldw3'].wait(te, 16 * (di + 1))
                    if hc == 0 and pair > 0:
                        S['v'].wait(te, pair_apply[pair - 1])
                    for tt in range(2):
                        t = pair * 2 + tt
                        for dg in range(4):
                            iw = te.matmul(px[tt * 4 + dg][:],
                                           hidT[:, hc * 1024 + t * 128:hc * 1024 + (t + 1) * 128],
                                           w3t[:, (di % 2) * 2048 + dg * 512:(di % 2) * 2048 + (dg + 1) * 512],
                                           start=(hc == 0), stop=(hc == H0C - 1))
                    w3_cons.append(S['te'].inc(iw))
                mm_done = w3_cons[-1]
                S['te'].wait(v, mm_done)
                S['te'].wait(sc, mm_done)
                for tt in range(2):
                    t = pair * 2 + tt
                    s0 = t * 24
                    for dg in range(4):
                        ist = v.bn_stats(stats[:, s0 + 6 * dg:s0 + 6 * dg + 6], px[tt * 4 + dg][:])
                        S['v'].wait(v, S['v'].inc(ist))
                    ia = v.bn_aggr(mvall[:, 2 * t:2 * t + 2], stats[:, s0:s0 + 24])
                    S['v'].wait(v, S['v'].inc(ia))
                    i1 = v.tensor_scalar(negm[:, t:t + 1], mvall[:, 2 * t:2 * t + 1], -1.0, None, ALU.mult)
                    nv = S['v'].inc(i1)
                    S['v'].wait(sc, nv)
                    i2 = sc.activation(sd[:, t:t + 1], mvall[:, 2 * t + 1:2 * t + 2], AF.Sqrt,
                                       bias=eps_tile[:, 0:1], scale=1.0)
                    sv = S['s'].inc(i2)
                    S['s'].wait(v, sv)
                    i3 = v.reciprocal(rstd[:, t:t + 1], sd[:, t:t + 1])
                    S['v'].wait(v, S['v'].inc(i3))
                    for dg in range(4):
                        v.tensor_scalar(xout[:, tt * D + dg * 512:tt * D + (dg + 1) * 512],
                                        px[tt * 4 + dg][:], negm[:, t:t + 1], rstd[:, t:t + 1],
                                        ALU.add, ALU.mult)
                    S['ldxr'].wait(v, 16 * (pair * 2 + tt + 1))
                    iadd = v.tensor_tensor(xout[:, tt * D:(tt + 1) * D],
                                           xout[:, tt * D:(tt + 1) * D],
                                           xin2[:, tt * D:(tt + 1) * D],
                                           op=ALU.add)
                    av = S['v'].inc(iadd)
                    S['v'].wait(g, av)
                    S['st'].inc(g.dma_start(out=x2_out[:, t * D:(t + 1) * D],
                                            in_=xout[:, tt * D:(tt + 1) * D]), 16)
                pair_apply.append(S['v'].n)
                S['st'].wait(v)
            S['st'].wait(g)
    return nc

def emit_ln_transpose(nc, S, v, sy, sc, te, x_in, xt, xlnt, xlnT3, sidb,
                      stats, mvall, negm, sd, rstd, eps_tile, ptrb,
                      ident_ld_val):
    ALUl = ALU
    apply_done = []
    evac_done = []
    base_ldx = S['ldx'].n
    for t in range(2):
        S['ldx'].inc(sy.dma_start(out=xt[:, (t % 2) * D:(t % 2 + 1) * D],
                                  in_=x_in[:, t * D:(t + 1) * D]), 16)
    for t in range(NT):
        tb = (t % 2) * D
        S['ldx'].wait(v, base_ldx + 16 * (t + 1))
        emit_ln(v, sc, S, lambda _t: xt[:, tb:tb + D], 1, D,
                stats[:, t * 24:], mvall[:, 2 * t:], negm[:, t:], sd[:, t:],
                rstd[:, t:], eps_tile)
        ia = v.tensor_scalar(xlnt[:, tb:tb + D], xt[:, tb:tb + D],
                             negm[:, t:t + 1], rstd[:, t:t + 1], ALUl.add, ALUl.mult)
        apply_done.append(S['v'].inc(ia))
        if t + 2 < NT:
            S['v'].wait(sy, apply_done[t])
            S['ldx'].inc(sy.dma_start(out=xt[:, ((t + 2) % 2) * D:((t + 2) % 2 + 1) * D],
                                      in_=x_in[:, (t + 2) * D:(t + 3) * D]), 16)
        for q in range(4):
            gi = t * 4 + q
            bank = ptrb[gi % 2]
            S['v'].wait(te, apply_done[-1])
            if gi >= 2:
                S['v'].wait(te, evac_done[gi - 2])
            if gi == 0:
                S['ld'].wait(te, ident_ld_val) if 'ld' in S and ident_ld_val else None
            for i in range(4):
                dc = q * 4 + i
                ins = te.transpose(bank[:, i * 128:(i + 1) * 128],
                                   xlnt[:, tb + dc * 128:tb + (dc + 1) * 128], sidb[:])
            tv = S['te'].inc(ins)
            S['te'].wait(v, tv)
            dst = xlnT3[:, 4 * q:4 * q + 4, t * 128:(t + 1) * 128]
            evac_done.append(S['v'].inc(v.tensor_copy(dst, bank[:])))
    return S['v'].n


def build_D(nc):
    x_in = nc.dram_tensor("x_img", [128, NT * D], F32, kind="ExternalInput").ap()
    qw_in = nc.dram_tensor("qw", [128, 16 * 2048], F32, kind="ExternalInput").ap()
    kw_in = nc.dram_tensor("kw", [128, 4 * 2048], F32, kind="ExternalInput").ap()
    vw_in = nc.dram_tensor("vw", [2048, 512], F32, kind="ExternalInput").ap()
    cos_in = nc.dram_tensor("cosd", [64, 1024], F32, kind="ExternalInput").ap()
    sin_in = nc.dram_tensor("sind", [64, 1024], F32, kind="ExternalInput").ap()
    identb_in = nc.dram_tensor("identb", [128, 128], BF16, kind="ExternalInput").ap()
    q_out = nc.dram_tensor("q_rot", [128, 16 * 1024], BF16, kind="ExternalOutput").ap()
    k_out = nc.dram_tensor("k_rot", [128, 4 * 1024], BF16, kind="ExternalOutput").ap()
    v_out = nc.dram_tensor("v_t", [128, 8 * 512], BF16, kind="ExternalOutput").ap()

    with contextlib.ExitStack() as es:
        sb = lambda n_, s_, d_: es.enter_context(nc.sbuf_tensor(n_, s_, d_))
        xt = sb("xtD", [128, 2 * D], F32)
        xlnt = sb("xlntD", [128, 2 * D], BF16)
        xlnT = sb("xlnTD", [128, 16 * 1024], BF16)
        sidb = sb("sidbD", [128, 128], BF16)
        eps_tile = sb("epsD", [128, 1], F32)
        stats = sb("statsD", [128, NT * 24], F32)
        mvall = sb("mvallD", [128, 16], F32)
        negm = sb("negmD", [128, 8], F32)
        sd = sb("sdD", [128, 8], F32)
        rstd = sb("rstdD", [128, 8], F32)
        wst = sb("wstD", [128, 2 * 2048], F32)
        wsb = sb("wsbD", [128, 2 * 2048], BF16)
        qf = sb("qfD", [128, 2 * 1024], F32)
        ct = sb("ctD", [128, 1024], F32)
        st_ = sb("stD", [128, 1024], F32)
        tt1 = sb("tt1D", [128, 2 * 1024], F32)
        tt2 = sb("tt2D", [128, 2 * 1024], F32)
        qsw = sb("qswD", [128, 2 * 1024], F32)
        sgn = sb("sgnD", [128, 1], F32)
        qro = sb("qroD", [128, 16 * 1024], BF16)
        kro = sb("kroD", [128, 4 * 1024], BF16)
        vwb = sb("vwbD", [128, 16 * 512], BF16)
        vt = sb("vtD", [128, 8 * 512], BF16)
        S = {k: Sem(es, nc, "semD_" + k) for k in
             ['ldx', 'ld', 'ldw', 'v', 's', 'te', 'st']}
        ptrb = [es.enter_context(nc.psum_tensor("ptrbD0", [128, 512], BF16)),
                es.enter_context(nc.psum_tensor("ptrbD1", [128, 512], BF16))]
        pp = [es.enter_context(nc.psum_tensor(f"ppD{i}", [128, 512], F32)) for i in range(4)]
        block = es.enter_context(nc.Block())

        @block.vector
        def _(v):
            sy, sc, te, g = nc.sync, nc.scalar, nc.tensor, nc.gpsimd
            xlnT3 = xlnT[:].rearrange("p (a c) -> p a c", c=1024)
            v.memset(eps_tile[:], EPS_LN)
            v.memset(sgn[0:64, :], -1.0)
            v.memset(sgn[64:128, :], 1.0)
            idv = S['ld'].inc(sy.dma_start(out=sidb[:], in_=identb_in[:, :]), 16)
            S['ld'].inc(sy.dma_start(out=ct[0:64, :], in_=cos_in[:, :]), 16)
            S['ld'].inc(sy.dma_start(out=ct[64:128, :], in_=cos_in[:, :]), 16)
            S['ld'].inc(sy.dma_start(out=st_[0:64, :], in_=sin_in[:, :]), 16)
            S['ld'].inc(sy.dma_start(out=st_[64:128, :], in_=sin_in[:, :]), 16)
            ld_all = S['ld'].n
            emit_ln_transpose(nc, S, v, sy, sc, te, x_in, xt, xlnt, xlnT3, sidb,
                              stats, mvall, negm, sd, rstd, eps_tile, ptrb, idv)
            xlnT_done = S['v'].n
            S['v'].wait(te, xlnT_done)
            S['ld'].wait(v, ld_all)
            cast_done = []
            rope_done = []
            evq = []
            mmq = []
            for ci in range(20):
                is_q = ci < 16
                src = qw_in if is_q else kw_in
                off = (ci if is_q else ci - 16) * 2048
                dstf = qro if is_q else kro
                dof = (ci if is_q else ci - 16) * 1024
                wb = (ci % 2)
                if ci >= 2:
                    S['v'].wait(sy, cast_done[ci - 2])
                S['ldw'].inc(sy.dma_start(out=wst[:, wb * 2048:(wb + 1) * 2048],
                                          in_=src[:, off:off + 2048]), 16)
                S['ldw'].wait(v, 16 * (ci + 1))
                icast = v.tensor_copy(wsb[:, wb * 2048:(wb + 1) * 2048],
                                      wst[:, wb * 2048:(wb + 1) * 2048])
                cast_done.append(S['v'].inc(icast))
                for tb in range(2):
                    pb = pp[(2 * ci + tb) % 4]
                    if 2 * ci + tb >= 4:
                        S['s'].wait(te, evq[2 * ci + tb - 4])
                    S['v'].wait(te, cast_done[-1])
                    for dc in range(16):
                        im = te.matmul(pb[:], wsb[:, wb * 2048 + dc * 128:wb * 2048 + (dc + 1) * 128],
                                       xlnT3[:, dc, tb * 512:(tb + 1) * 512],
                                       start=(dc == 0), stop=(dc == 15))
                    mmq.append(S['te'].inc(im))
                    S['te'].wait(sc, mmq[-1])
                    if ci >= 2 and tb == 0:
                        S['v'].wait(sc, rope_done[ci - 2])
                    qb = (ci % 2) * 1024
                    iev = sc.activation(qf[:, qb + tb * 512:qb + (tb + 1) * 512], pb[:],
                                        AF.Copy, bias=0.0, scale=(QSCALE if is_q else 1.0))
                    evq.append(S['s'].inc(iev))
                S['s'].wait(v, evq[-1])
                if ci >= 2:
                    pass
                qb = (ci % 2) * 1024
                b = (ci % 2) * 1024
                v.tensor_copy(qsw[0:64, b:b + 1024], qf[64:128, qb:qb + 1024])
                v.tensor_copy(qsw[64:128, b:b + 1024], qf[0:64, qb:qb + 1024])
                v.tensor_tensor(tt1[:, b:b + 1024], qf[:, qb:qb + 1024], ct[:], op=ALU.mult)
                v.tensor_tensor(tt2[:, b:b + 1024], qsw[:, b:b + 1024], st_[:], op=ALU.mult)
                ir = v.scalar_tensor_tensor(dstf[:, dof:dof + 1024], tt2[:, b:b + 1024],
                                            sgn[:, 0:1], tt1[:, b:b + 1024],
                                            ALU.mult, ALU.add)
                rope_done.append(S['v'].inc(ir))
            for dc in range(16):
                wb = dc % 2
                if dc < 2:
                    S['v'].wait(sy, cast_done[18 + dc])
                else:
                    S['v'].wait(sy, cast_done[20 + dc - 2])
                S['ldw'].inc(sy.dma_start(out=wst[:, wb * 2048:wb * 2048 + 512],
                                          in_=vw_in[dc * 128:(dc + 1) * 128, :]), 16)
                S['ldw'].wait(v, 16 * (21 + dc))
                ic = v.tensor_copy(vwb[:, dc * 512:(dc + 1) * 512],
                                   wst[:, wb * 2048:wb * 2048 + 512])
                cast_done.append(S['v'].inc(ic))
            S['v'].wait(te)
            S['s'].wait(te)
            evv = []
            for t in range(NT):
                pb = pp[t % 4]
                if t >= 4:
                    S['v'].wait(te, evv[t - 4])
                for dc in range(16):
                    im = te.matmul(pb[:], xlnT3[:, dc, t * 128:(t + 1) * 128],
                                   vwb[:, dc * 512:(dc + 1) * 512],
                                   start=(dc == 0), stop=(dc == 15))
                mv_ = S['te'].inc(im)
                S['te'].wait(v, mv_)
                iev = v.tensor_copy(vt[:, t * 512:(t + 1) * 512], pb[:])
                evv.append(S['v'].inc(iev))
            S['v'].wait(g)
            S['st'].inc(g.dma_start(out=q_out[:, :], in_=qro[:]), 16)
            S['st'].inc(g.dma_start(out=k_out[:, :], in_=kro[:]), 16)
            S['st'].inc(g.dma_start(out=v_out[:, :], in_=vt[:]), 16)
            S['st'].wait(g)
    return nc


def build_E(nc):
    x_in = nc.dram_tensor("x_img", [128, NT * D], F32, kind="ExternalInput").ap()
    q_in = nc.dram_tensor("q_rot", [128, 16 * 1024], BF16, kind="ExternalInput").ap()
    k_in = nc.dram_tensor("k_all", [128, 4 * T], BF16, kind="ExternalInput").ap()
    v_in = nc.dram_tensor("v_all", [T, 512], BF16, kind="ExternalInput").ap()
    cw_in = nc.dram_tensor("cw", [2048, 2048], F32, kind="ExternalInput").ap()
    mask_in = nc.dram_tensor("mask", [128, 512], F32, kind="ExternalInput").ap()
    identb_in = nc.dram_tensor("identb", [128, 128], BF16, kind="ExternalInput").ap()
    ident32_in = nc.dram_tensor("ident32", [128, 128], F32, kind="ExternalInput").ap()
    x3_out = nc.dram_tensor("x3", [128, NT * D], F32, kind="ExternalOutput").ap()

    with contextlib.ExitStack() as es:
        sb = lambda n_, s_, d_: es.enter_context(nc.sbuf_tensor(n_, s_, d_))
        attn_tok = sb("attok", [128, NT * 2048], BF16)
        sidb = sb("sidbE", [128, 128], BF16)
        sid32 = sb("sid32E", [128, 128], F32)
        eps_tile = sb("epsE", [128, 1], F32)
        ones_cb = sb("onescbE", [128, 1], BF16)
        stats = sb("statsE", [128, NT * 24], F32)
        mvall = sb("mvallE", [128, 16], F32)
        negmt = sb("negmtE", [128, 8], F32)
        sd = sb("sdE", [128, 8], F32)
        rstd = sb("rstdE", [128, 8], F32)
        negmb = sb("negmbE", [128, 16], F32)
        S = {k: Sem(es, nc, "semE_" + k) for k in
             ['ld', 'ldc', 'ldxr', 'v', 's', 'te', 'g', 'st']}
        nb_d = nc.dram_tensor("nb_d", [1, 16], F32, kind="Internal").ap()

        es1 = contextlib.ExitStack()
        sb1 = lambda n_, s_, d_: es1.enter_context(nc.sbuf_tensor(n_, s_, d_))
        k_sb = sb1("ksbE", [128, 4 * T], BF16)
        v_sb = sb1("vsbE", [128, 32 * 512], BF16)
        q_sb = sb1("qsbE", [128, 16 * 1024], BF16)
        ksq = sb1("ksqE", [128, 2 * T], BF16)
        P_sb = sb1("PsbE", [128, 2 * 512], BF16)
        msk = sb1("mskE", [128, 512], F32)
        att_s = sb1("attsE", [128, 128], BF16)
        l_sb = sb1("lsbE", [1, 128], F32)
        rl_col = sb1("rlcolE", [128, 1], F32)
        nm_row = sb1("nmrowE", [1, 40], F32)
        kn = sb1("knE", [1, 4], F32)
        qn = sb1("qnE", [1, 16], F32)
        knr = sb1("knrE", [1, 16], F32)
        nmr = sb1("nmrE", [1, 16], F32)
        ps1 = contextlib.ExitStack()
        pst = [ps1.enter_context(nc.psum_tensor(f"pstE{i}", [128, 512], F32)) for i in range(2)]
        ppv = ps1.enter_context(nc.psum_tensor("ppvE", [128, 128], F32))
        pl = ps1.enter_context(nc.psum_tensor("plE", [1, 128], F32))
        ptr = ps1.enter_context(nc.psum_tensor("ptrE", [128, 128], BF16))
        ptr32 = ps1.enter_context(nc.psum_tensor("ptr32E", [128, 128], F32))
        pn = ps1.enter_context(nc.psum_tensor("pnE", [1, 512], F32))
        block = es.enter_context(nc.Block())

        @block.vector
        def _(v):
            sy, sc, te, g = nc.sync, nc.scalar, nc.tensor, nc.gpsimd
            v.memset(eps_tile[:], EPS_LN)
            v.memset(ones_cb[:], 1.0)
            S['ld'].inc(sy.dma_start(out=sidb[:], in_=identb_in[:, :]), 16)
            S['ld'].inc(sy.dma_start(out=sid32[:], in_=ident32_in[:, :]), 16)
            S['ld'].inc(sy.dma_start(out=q_sb[:], in_=q_in[:, :]), 16)
            S['ld'].inc(sy.dma_start(out=k_sb[:], in_=k_in[:, :]), 16)
            S['ld'].inc(sy.dma_start(out=v_sb[:], in_=v_in.rearrange("(c p) d -> p c d", p=128)), 16)
            S['ld'].inc(sy.dma_start(out=msk[:], in_=mask_in[:, :]), 16)
            ld_all = S['ld'].n
            S['ld'].wait(v, ld_all)
            S['ld'].wait(sc, ld_all)
            S['ld'].wait(te, ld_all)
            for gi in range(4):
                if gi >= 2:
                    S['v'].wait(sc)
                isq = sc.activation(ksq[:, (gi % 2) * T:(gi % 2 + 1) * T],
                                    k_sb[:, gi * T:(gi + 1) * T],
                                    AF.Square, bias=0.0, scale=1.0)
                sqv = S['s'].inc(isq)
                S['s'].wait(te, sqv)
                for c in range(8):
                    imm = te.matmul(pn[:], ones_cb[:],
                                    ksq[:, (gi % 2) * T + c * 512:(gi % 2) * T + (c + 1) * 512],
                                    start=True, stop=True)
                    tv = S['te'].inc(imm)
                    S['te'].wait(v, tv)
                    ir = v.tensor_reduce(nm_row[:, 8 * (gi % 2) + c:8 * (gi % 2) + c + 1],
                                         pn[0:1, :], axis=AX.X, op=ALU.max)
                    S['v'].wait(v, S['v'].inc(ir))
                    S['v'].wait(te)
                ir = v.tensor_reduce(kn[:, gi:gi + 1], nm_row[:, 8 * (gi % 2):8 * (gi % 2) + 8],
                                     axis=AX.X, op=ALU.max)
                S['v'].wait(v, S['v'].inc(ir))
            for h in range(NHEAD):
                isq = sc.activation(ksq[:, (h % 4) * 1024:(h % 4) * 1024 + 1024],
                                    q_sb[:, h * 1024:(h + 1) * 1024], AF.Square,
                                    bias=0.0, scale=1.0)
                hv = S['s'].inc(isq)
                if h >= 1:
                    pass
                S['s'].wait(te, hv)
                for c in range(2):
                    imm = te.matmul(pn[:], ones_cb[:],
                                    ksq[:, (h % 4) * 1024 + c * 512:(h % 4) * 1024 + (c + 1) * 512],
                                    start=True, stop=True)
                    tv = S['te'].inc(imm)
                    S['te'].wait(v, tv)
                    ir = v.tensor_reduce(nm_row[:, 32 + c:33 + c], pn[0:1, :], axis=AX.X, op=ALU.max)
                    S['v'].wait(v, S['v'].inc(ir))
                    S['v'].wait(te)
                ir = v.tensor_reduce(qn[:, h:h + 1], nm_row[:, 32:34], axis=AX.X, op=ALU.max)
                S['v'].wait(v, S['v'].inc(ir))
                S['v'].wait(sc)
            vv = S['v'].n
            S['v'].wait(sc, vv)
            sc.activation(kn[:], kn[:], AF.Sqrt, bias=eps_tile[0:1, 0:1], scale=1.0)
            iq = sc.activation(qn[:], qn[:], AF.Sqrt, bias=eps_tile[0:1, 0:1], scale=1.0)
            sv = S['s'].inc(iq)
            S['s'].wait(v, sv)
            for h in range(NHEAD):
                v.tensor_copy(knr[:, h:h + 1], kn[:, h // 4:h // 4 + 1])
            iv = v.tensor_tensor(nmr[:], qn[:], knr[:], op=ALU.mult)
            S['v'].inc(iv)
            iv2 = v.tensor_scalar(nmr[:], nmr[:], -1.05, None, ALU.mult)
            nv = S['v'].inc(iv2)
            S['v'].wait(g, nv)
            gv = S['g'].inc(g.dma_start(out=nb_d[0:1, :], in_=nmr[:]), 16)
            S['g'].wait(g, gv)
            gv = S['g'].inc(g.dma_start(out=negmb[:],
                                        in_=nb_d[0:1, :].squeeze(0).partition_broadcast(128)), 16)
            S['g'].wait(sc, gv)
            S['g'].wait(v, gv)
            exp_done = {}
            pv_done = {}
            row_free = []
            prev_pv = None
            for h in range(NHEAD):
                gi = h // 4
                for j in range(NT):
                    nb = j + 1
                    qv_ap = q_sb[:, h * 1024 + j * 128:h * 1024 + (j + 1) * 128]
                    for bg in range(nb):
                        stb = pst[bg % 2]
                        if bg >= 2:
                            S['s'].wait(te, exp_done[(h, j, bg - 2)])
                        if (h, j) != (0, 0) and bg == 0:
                            S['v'].wait(te, row_free[-1])
                        for sub in range(4):
                            ktc = bg * 4 + sub
                            ist = te.matmul(stb[:, sub * 128:(sub + 1) * 128],
                                            k_sb[:, gi * T + ktc * 128:gi * T + (ktc + 1) * 128],
                                            qv_ap, start=True, stop=True)
                        stv = S['te'].inc(ist)
                        if bg == nb - 1:
                            S['te'].wait(v, stv)
                            imk = v.tensor_tensor(stb[:], stb[:], msk[:], op=ALU.add)
                            mkv = S['v'].inc(imk)
                            S['v'].wait(sc, mkv)
                        else:
                            S['te'].wait(sc, stv)
                        if bg >= 2:
                            S['te'].wait(sc, pv_done[(h, j, bg - 2)])
                        elif prev_pv is not None:
                            S['te'].wait(sc, prev_pv)
                        iex = sc.activation(P_sb[:, (bg % 2) * 512:(bg % 2 + 1) * 512], stb[:],
                                            AF.Exp, bias=negmb[:, h:h + 1], scale=1.0)
                        exp_done[(h, j, bg)] = S['s'].inc(iex)
                        S['s'].wait(te, exp_done[(h, j, bg)])
                        for sub in range(4):
                            ktc = bg * 4 + sub
                            first = (bg == 0 and sub == 0)
                            last = (bg == nb - 1 and sub == 3)
                            te.matmul(ppv[:], v_sb[:, ktc * 512 + gi * 128:ktc * 512 + (gi + 1) * 128],
                                      P_sb[:, (bg % 2) * 512 + sub * 128:(bg % 2) * 512 + (sub + 1) * 128],
                                      start=first, stop=last)
                            ipl = te.matmul(pl[:], ones_cb[:],
                                            P_sb[:, (bg % 2) * 512 + sub * 128:(bg % 2) * 512 + (sub + 1) * 128],
                                            start=first, stop=last)
                        pv_done[(h, j, bg)] = S['te'].inc(ipl)
                    prev_pv = pv_done[(h, j, nb - 1)]
                    rowv = pv_done[(h, j, nb - 1)]
                    S['te'].wait(v, rowv)
                    il = v.tensor_scalar(l_sb[:], pl[0:1, :], 1e-30, None, ALU.add)
                    S['v'].wait(v, S['v'].inc(il))
                    il2 = v.reciprocal(l_sb[:], l_sb[:])
                    rv = S['v'].inc(il2)
                    S['v'].wait(te, rv)
                    itr = te.transpose(ptr32[:, 0:1], l_sb[0:1, :], sid32[0:1, 0:1])
                    t1 = S['te'].inc(itr)
                    S['te'].wait(v, t1)
                    ir1 = v.tensor_copy(rl_col[:], ptr32[:, 0:1])
                    ie1 = v.tensor_copy(att_s[:], ppv[:])
                    ev = S['v'].inc(ie1)
                    S['v'].wait(v, ev)
                    S['v'].wait(te, ev)
                    itr2 = te.transpose(ptr[:], att_s[:], sidb[:])
                    t2 = S['te'].inc(itr2)
                    S['te'].wait(v, t2)
                    ifin = v.tensor_scalar(attn_tok[:, j * 2048 + h * 128:j * 2048 + (h + 1) * 128],
                                           ptr[:], rl_col[:, 0:1], None, ALU.mult)
                    row_free.append(S['v'].inc(ifin))
            att_done = S['v'].n
            ps1.close()
            es1.close()
            attn_T = es.enter_context(nc.sbuf_tensor("attTE", [128, 16 * 1024], BF16))
            c_wb = es.enter_context(nc.sbuf_tensor("cwbE", [128, 16 * 2048], BF16))
            cst = es.enter_context(nc.sbuf_tensor("cstE", [128, 2 * 2048], F32))
            xout = es.enter_context(nc.sbuf_tensor("xoutE", [128, 2048], F32))
            xin2 = es.enter_context(nc.sbuf_tensor("xin2E", [128, 2048], F32))
            ptrb2 = [es.enter_context(nc.psum_tensor(f"ptrb2E{i}", [128, 512], BF16)) for i in range(2)]
            pog = [es.enter_context(nc.psum_tensor(f"pogE{i}", [128, 512], F32)) for i in range(4)]
            attn_T3 = attn_T[:].rearrange("p (a c) -> p a c", c=1024)
            ccast = []
            for fc in range(16):
                if fc == 0:
                    S['v'].wait(sy, att_done)
                if fc >= 2:
                    S['v'].wait(sy, ccast[fc - 2])
                S['ldc'].inc(sy.dma_start(out=cst[:, (fc % 2) * 2048:(fc % 2 + 1) * 2048],
                                          in_=cw_in[fc * 128:(fc + 1) * 128, :]), 16)
                S['ldc'].wait(v, 16 * (fc + 1))
                ic = v.tensor_copy(c_wb[:, fc * 2048:(fc + 1) * 2048],
                                   cst[:, (fc % 2) * 2048:(fc % 2 + 1) * 2048])
                ccast.append(S['v'].inc(ic))
            S['v'].wait(te, att_done)
            evac_done = []
            for t in range(NT):
                for q in range(4):
                    gidx = t * 4 + q
                    bank = ptrb2[gidx % 2]
                    if gidx >= 2:
                        S['v'].wait(te, evac_done[gidx - 2])
                    for i in range(4):
                        dc = q * 4 + i
                        ins = te.transpose(bank[:, i * 128:(i + 1) * 128],
                                           attn_tok[:, t * 2048 + dc * 128:t * 2048 + (dc + 1) * 128],
                                           sidb[:])
                    tv = S['te'].inc(ins)
                    S['te'].wait(v, tv)
                    dst = attn_T3[:, 4 * q:4 * q + 4, t * 128:(t + 1) * 128]
                    evac_done.append(S['v'].inc(v.tensor_copy(dst, bank[:])))
            aT_done = S['v'].n
            S['v'].wait(te, aT_done)
            prev_apply = None
            for t in range(NT):
                if prev_apply is not None:
                    S['v'].wait(sy, prev_apply)
                S['ldxr'].inc(sy.dma_start(out=xin2[:], in_=x_in[:, t * D:(t + 1) * D]), 16)
                if prev_apply is not None:
                    S['v'].wait(te, prev_apply)
                for fc in range(16):
                    for og in range(4):
                        imm = te.matmul(pog[og][:], attn_T3[:, fc, t * 128:(t + 1) * 128],
                                        c_wb[:, fc * 2048 + og * 512:fc * 2048 + (og + 1) * 512],
                                        start=(fc == 0), stop=(fc == 15))
                mmv = S['te'].inc(imm)
                S['te'].wait(v, mmv)
                S['te'].wait(sc, mmv)
                s0 = t * 24
                for og in range(4):
                    ist = v.bn_stats(stats[:, s0 + 6 * og:s0 + 6 * og + 6], pog[og][:])
                    S['v'].wait(v, S['v'].inc(ist))
                ia = v.bn_aggr(mvall[:, 2 * t:2 * t + 2], stats[:, s0:s0 + 24])
                S['v'].wait(v, S['v'].inc(ia))
                i1 = v.tensor_scalar(negmt[:, t:t + 1], mvall[:, 2 * t:2 * t + 1], -1.0, None, ALU.mult)
                nv2 = S['v'].inc(i1)
                S['v'].wait(sc, nv2)
                i2 = sc.activation(sd[:, t:t + 1], mvall[:, 2 * t + 1:2 * t + 2], AF.Sqrt,
                                   bias=eps_tile[:, 0:1], scale=1.0)
                sv2 = S['s'].inc(i2)
                S['s'].wait(v, sv2)
                i3 = v.reciprocal(rstd[:, t:t + 1], sd[:, t:t + 1])
                S['v'].wait(v, S['v'].inc(i3))
                for og in range(4):
                    v.tensor_scalar(xout[:, og * 512:(og + 1) * 512], pog[og][:],
                                    negmt[:, t:t + 1], rstd[:, t:t + 1], ALU.add, ALU.mult)
                S['ldxr'].wait(v, 16 * (t + 1))
                iadd = v.tensor_tensor(xout[:], xout[:], xin2[:], op=ALU.add)
                av = S['v'].inc(iadd)
                prev_apply = av
                S['v'].wait(g, av)
                S['st'].inc(g.dma_start(out=x3_out[:, t * D:(t + 1) * D], in_=xout[:]), 16)
                S['st'].wait(v)
            S['st'].wait(g)
    return nc


def build_F(nc):
    x_in = nc.dram_tensor("x_img", [128, NT * D], F32, kind="ExternalInput").ap()
    wg_in = nc.dram_tensor("wg", [256 * 128, 512], F32, kind="ExternalInput").ap()
    wu_in = nc.dram_tensor("wu", [256 * 128, 512], F32, kind="ExternalInput").ap()
    wd_in = nc.dram_tensor("wd", [HI, D], F32, kind="ExternalInput").ap()
    identb_in = nc.dram_tensor("identb", [128, 128], BF16, kind="ExternalInput").ap()
    xo_out = nc.dram_tensor("xo", [128, NT * D], F32, kind="ExternalOutput").ap()
    wdb_d = nc.dram_tensor("wdb_d", [HI, D], BF16, kind="Internal").ap()

    with contextlib.ExitStack() as es:
        sb = lambda n_, s_, d_: es.enter_context(nc.sbuf_tensor(n_, s_, d_))
        xlnT = sb("xlnTF", [128, 16 * 1024], BF16)
        hs = sb("hsF", [128, 64 * 1024], BF16)
        sidb = sb("sidbF", [128, 128], BF16)
        eps_tile = sb("epsF", [128, 1], F32)
        stats = sb("statsF", [128, NT * 24], F32)
        mvall = sb("mvallF", [128, 16], F32)
        negm = sb("negmF", [128, 8], F32)
        sd = sb("sdF", [128, 8], F32)
        rstd = sb("rstdF", [128, 8], F32)
        S = {k: Sem(es, nc, "semF_" + k) for k in
             ['ldx', 'ldw', 'ldwd', 'ldxr', 'v', 's', 'te', 'g', 'st']}
        es0 = contextlib.ExitStack()
        xt = es0.enter_context(nc.sbuf_tensor("xtF", [128, 2 * D], F32))
        xlnt = es0.enter_context(nc.sbuf_tensor("xlntF", [128, 2 * D], BF16))
        ptrb = [es0.enter_context(nc.psum_tensor(f"ptrbF{i}", [128, 512], BF16)) for i in range(2)]
        es1 = contextlib.ExitStack()
        block = es.enter_context(nc.Block())

        @block.vector
        def _(v):
            sy, sc, te, g = nc.sync, nc.scalar, nc.tensor, nc.gpsimd
            xlnT3 = xlnT[:].rearrange("p (a c) -> p a c", c=1024)
            v.memset(eps_tile[:], EPS_LN)
            idv = S['ldx'].inc(sy.dma_start(out=sidb[:], in_=identb_in[:, :]), 16)
            S['ldx'].wait(te, idv)
            emit_ln_transpose(nc, S, v, sy, sc, te, x_in, xt, xlnt, xlnT3, sidb,
                              stats, mvall, negm, sd, rstd, eps_tile, ptrb, 0)
            xlnT_done = S['v'].n
            S['v'].wait(te, xlnT_done)
            es0.close()
            sb1 = lambda n_, s_, d_: es1.enter_context(nc.sbuf_tensor(n_, s_, d_))
            wst = sb1("wstF", [128, 2 * 512], F32)
            wbt = sb1("wbtF", [128, 2 * 512], BF16)
            gch = sb1("gchF", [128, 8 * 512], F32)
            ht = sb1("htF", [128, 8 * 512], BF16)
            at = sb1("atF", [128, 8 * 512], BF16)
            cs_ = sb1("csF", [128, 6 * 512], BF16)
            ts_ = sb1("tsF", [128, 2 * 512], BF16)
            pp = [es.enter_context(nc.psum_tensor(f"ppF{i}", [128, 512], F32)) for i in range(8)]
            S['v'].wait(sy, xlnT_done)
            wcast = []
            wcons = []
            ev_done = []
            sp_done = []
            nds = 0
            for gq in range(16):
                for sweep, wsrc in ((0, wg_in), (1, wu_in)):
                    for dc in range(16):
                        di = len(wcast)
                        if di >= 2:
                            S['v'].wait(sy, wcast[di - 2])
                        S['ldw'].inc(sy.dma_start(
                            out=wst[:, (di % 2) * 512:(di % 2 + 1) * 512],
                            in_=wsrc[(gq * 16 + dc) * 128:(gq * 16 + dc + 1) * 128, :]), 16)
                        S['ldw'].wait(v, 16 * (di + 1))
                        if di >= 2:
                            S['te'].wait(v, wcons[di - 2])
                        ic = v.tensor_copy(wbt[:, (di % 2) * 512:(di % 2 + 1) * 512],
                                           wst[:, (di % 2) * 512:(di % 2 + 1) * 512])
                        wcast.append(S['v'].inc(ic))
                        S['v'].wait(te, wcast[-1])
                        if dc == 0 and ev_done:
                            S['v'].wait(te, ev_done[-1][0])
                            S['s'].wait(te, ev_done[-1][1])
                        for tb in range(2):
                            for sc_ in range(4):
                                imm = te.matmul(
                                    pp[tb * 4 + sc_][:],
                                    wbt[:, (di % 2) * 512 + sc_ * 128:(di % 2) * 512 + (sc_ + 1) * 128],
                                    xlnT3[:, dc, tb * 512:(tb + 1) * 512],
                                    start=(dc == 0), stop=(dc == 15))
                        wcons.append(S['te'].inc(imm))
                    mmv = wcons[-1]
                    S['te'].wait(sc, mmv)
                    S['te'].wait(v, mmv)
                    if sweep == 0:
                        for i8 in range(8):
                            if gq > 0 and i8 == 0:
                                S['v'].wait(sc, sp_done[-1])
                            isl = sc.activation(gch[:, i8 * 512:(i8 + 1) * 512], pp[i8][:],
                                                AF.Silu, bias=0.0, scale=1.0)
                        slv = S['s'].inc(isl)
                        ev_done.append((S['v'].n, slv))
                    else:
                        S['s'].wait(v)
                        for i8 in range(8):
                            iht = v.scalar_tensor_tensor(ht[:, i8 * 512:(i8 + 1) * 512],
                                                         pp[i8][:], 1.0, gch[:, i8 * 512:(i8 + 1) * 512],
                                                         ALU.mult, ALU.mult)
                        hv = S['v'].inc(iht)
                        S['v'].wait(sc, hv)
                        for i8 in range(8):
                            iab = sc.activation(at[:, i8 * 512:(i8 + 1) * 512], ht[:, i8 * 512:(i8 + 1) * 512],
                                                AF.Abs, bias=0.0, scale=1.0)
                        abv = S['s'].inc(iab)
                        S['s'].wait(v, abv)
                        ev_done.append((S['v'].n, abv))
                        for tb in range(2):
                            a_ = [at[:, (tb * 4 + p) * 512:(tb * 4 + p + 1) * 512] for p in range(4)]
                            h_ = [ht[:, (tb * 4 + p) * 512:(tb * 4 + p + 1) * 512] for p in range(4)]
                            c_ = [cs_[:, i * 512:(i + 1) * 512] for i in range(6)]
                            t_ = [ts_[:, i * 512:(i + 1) * 512] for i in range(2)]
                            pairs = [(0, 1), (0, 2), (0, 3), (1, 2), (1, 3), (2, 3)]
                            for ci, (a, b) in enumerate(pairs):
                                v.tensor_tensor(c_[ci], a_[a], a_[b], op=ALU.is_ge)
                            def dst(p):
                                return hs[:, ((p * 16 + gq) * 1024 + tb * 512):((p * 16 + gq) * 1024 + (tb + 1) * 512)]
                            v.tensor_tensor(t_[0], c_[0], c_[1], op=ALU.add)
                            v.tensor_tensor(t_[1], t_[0], c_[2], op=ALU.add)
                            v.scalar_tensor_tensor(dst(0), t_[1], 2.0, h_[0], ALU.is_ge, ALU.mult)
                            v.tensor_tensor(t_[0], c_[3], c_[4], op=ALU.add)
                            v.tensor_tensor(t_[1], t_[0], c_[0], op=ALU.subtract)
                            v.scalar_tensor_tensor(dst(1), t_[1], 1.0, h_[1], ALU.is_ge, ALU.mult)
                            v.tensor_tensor(t_[0], c_[5], c_[1], op=ALU.subtract)
                            v.tensor_tensor(t_[1], t_[0], c_[3], op=ALU.subtract)
                            v.scalar_tensor_tensor(dst(2), t_[1], 0.0, h_[2], ALU.is_ge, ALU.mult)
                            v.tensor_tensor(t_[0], c_[2], c_[4], op=ALU.add)
                            v.tensor_tensor(t_[1], t_[0], c_[5], op=ALU.add)
                            isp = v.scalar_tensor_tensor(dst(3), t_[1], 1.0, h_[3], ALU.is_le, ALU.mult)
                        sp_done.append(S['v'].inc(isp))
            hs_done = S['v'].n
            es1.close()
            wdt = es.enter_context(nc.sbuf_tensor("wdtF", [128, 2 * D], F32))
            wdb = es.enter_context(nc.sbuf_tensor("wdbF", [128, 2 * D], BF16))
            xout = es.enter_context(nc.sbuf_tensor("xoutF", [128, D], F32))
            xin2 = es.enter_context(nc.sbuf_tensor("xin2F", [128, D], F32))
            S['v'].wait(te, hs_done)
            S['v'].wait(sy, hs_done)
            pair_apply = []
            wd_cons = []
            wd_cast = []
            for pair in range(4):
                if pair >= 1:
                    S['v'].wait(sy, pair_apply[-1])
                for ic in range(64):
                    di = pair * 64 + ic
                    if pair == 0:
                        if di >= 2:
                            S['v'].wait(sy, wd_cast[di - 2])
                        S['ldwd'].inc(sy.dma_start(out=wdt[:, (di % 2) * D:(di % 2 + 1) * D],
                                                   in_=wd_in[ic * 128:(ic + 1) * 128, :]), 16)
                        S['ldwd'].wait(v, 16 * (di + 1))
                        if di >= 2:
                            S['te'].wait(v, wd_cons[di - 2])
                            S['st'].wait(v, 16 * (di - 1))
                        icw = v.tensor_copy(wdb[:, (di % 2) * D:(di % 2 + 1) * D],
                                            wdt[:, (di % 2) * D:(di % 2 + 1) * D])
                        wd_cast.append(S['v'].inc(icw))
                        S['v'].wait(g, wd_cast[-1])
                        S['st'].inc(g.dma_start(out=wdb_d[ic * 128:(ic + 1) * 128, :],
                                                in_=wdb[:, (di % 2) * D:(di % 2 + 1) * D]), 16)
                        S['v'].wait(te, wd_cast[-1])
                    else:
                        if di >= 2:
                            S['te'].wait(sy, wd_cons[di - 2])
                        if pair == 1 and ic < 2:
                            S['st'].wait(sy)
                        S['ldwd'].inc(sy.dma_start(out=wdb[:, (di % 2) * D:(di % 2 + 1) * D],
                                                   in_=wdb_d[ic * 128:(ic + 1) * 128, :]), 16)
                        S['ldwd'].wait(te, 16 * (di + 1))
                    if ic == 0 and pair > 0:
                        S['v'].wait(te, pair_apply[-1])
                    for tt in range(2):
                        t = pair * 2 + tt
                        for dg in range(4):
                            iw = te.matmul(pp[tt * 4 + dg][:],
                                           hs[:, ic * 1024 + t * 128:ic * 1024 + (t + 1) * 128],
                                           wdb[:, (di % 2) * D + dg * 512:(di % 2) * D + (dg + 1) * 512],
                                           start=(ic == 0), stop=(ic == 63))
                    wd_cons.append(S['te'].inc(iw))
                mm_done = wd_cons[-1]
                S['te'].wait(v, mm_done)
                S['te'].wait(sc, mm_done)
                for tt in range(2):
                    t = pair * 2 + tt
                    S['v'].wait(sy)
                    S['ldxr'].inc(sy.dma_start(out=xin2[:], in_=x_in[:, t * D:(t + 1) * D]), 16)
                    s0 = t * 24
                    for dg in range(4):
                        ist = v.bn_stats(stats[:, s0 + 6 * dg:s0 + 6 * dg + 6], pp[tt * 4 + dg][:])
                        S['v'].wait(v, S['v'].inc(ist))
                    ia = v.bn_aggr(mvall[:, 2 * t:2 * t + 2], stats[:, s0:s0 + 24])
                    S['v'].wait(v, S['v'].inc(ia))
                    i1 = v.tensor_scalar(negm[:, t:t + 1], mvall[:, 2 * t:2 * t + 1], -1.0, None, ALU.mult)
                    nv = S['v'].inc(i1)
                    S['v'].wait(sc, nv)
                    i2 = sc.activation(sd[:, t:t + 1], mvall[:, 2 * t + 1:2 * t + 2], AF.Sqrt,
                                       bias=eps_tile[:, 0:1], scale=1.0)
                    sv = S['s'].inc(i2)
                    S['s'].wait(v, sv)
                    i3 = v.reciprocal(rstd[:, t:t + 1], sd[:, t:t + 1])
                    S['v'].wait(v, S['v'].inc(i3))
                    S['st'].wait(v)
                    for dg in range(4):
                        v.tensor_scalar(xout[:, dg * 512:(dg + 1) * 512],
                                        pp[tt * 4 + dg][:], negm[:, t:t + 1], rstd[:, t:t + 1],
                                        ALU.add, ALU.mult)
                    S['ldxr'].wait(v)
                    iadd = v.tensor_tensor(xout[:], xout[:], xin2[:], op=ALU.add)
                    av = S['v'].inc(iadd)
                    S['v'].wait(g, av)
                    S['st'].inc(g.dma_start(out=xo_out[:, t * D:(t + 1) * D], in_=xout[:]), 16)
                pair_apply.append(S['v'].n)
            S['st'].wait(g)
    return nc



_ROPE_PERM1 = np.concatenate([np.arange(0, 128, 2), np.arange(1, 128, 2)])

def _img_lhsT(wT, nout, dtype):
    out = np.empty((128, nout * 2048), dtype)
    w4 = np.asarray(wT).reshape(16, 128, nout, 128)
    for oc in range(nout):
        out[:, oc * 2048:(oc + 1) * 2048] = w4[:, :, oc, :].transpose(1, 0, 2).reshape(128, 2048)
    return np.ascontiguousarray(out)

def _prep_all(inputs):
    BFt = BF
    x = np.asarray(inputs["L0_attn_x"], np.float32).reshape(2 * T, D)
    Wkv = np.asarray(inputs["L0_attn_Wkv"], np.float32)
    bkv = np.asarray(inputs["L0_attn_bkv"], np.float32)
    vcols = np.concatenate([np.arange(g * 128 + 64, g * 128 + 128) for g in range(4)])
    wkvv = wimg(np.ascontiguousarray(Wkv[:, vcols]))
    bkvv = np.ascontiguousarray(bkv[vcols]).reshape(1, 256)
    identb = np.eye(128, dtype=BFt)
    ident32 = np.eye(128, dtype=np.float32)
    w1T = np.ascontiguousarray(np.asarray(inputs["L0_ffn_w1"], np.float32).T)
    w2T = np.ascontiguousarray(np.asarray(inputs["L0_ffn_w2"], np.float32).T)
    w3T = np.ascontiguousarray(np.asarray(inputs["L0_ffn_w3"], np.float32).T)
    q_w = np.asarray(inputs["L1_attn_q_w"], np.float32)
    kv_w = np.asarray(inputs["L1_attn_kv_w"], np.float32)
    qperm = np.concatenate([h * 128 + _ROPE_PERM1 for h in range(16)])
    kperm = np.concatenate([g * 128 + _ROPE_PERM1 for g in range(4)])
    qwT = np.ascontiguousarray(q_w[qperm].T)
    kwT = np.ascontiguousarray(kv_w[0:512][kperm].T)
    vwT = np.ascontiguousarray(kv_w[512:1024].T)
    cwT = np.ascontiguousarray(np.asarray(inputs["L1_attn_c_w"], np.float32).T)
    pairs = np.arange(64, dtype=np.float32)
    freqs = 1.0 / 10000.0 ** (2.0 * pairs / 128.0)
    pos = np.arange(T, dtype=np.float32)
    ang = pos[:, None] * freqs[None, :]
    cosf = np.cos(ang).astype(np.float32)
    sinf = np.sin(ang).astype(np.float32)
    jj = np.arange(HI)
    gq = jj // 512; rem = jj % 512; sc = rem // 128; cc = rem % 128
    perm_cols = 4 * (gq * 128 + cc) + sc
    ic = jj // 128; c2 = jj % 128
    perm_rows = 4 * ((ic % 16) * 128 + c2) + ic // 16
    wg_p = np.asarray(inputs["L1_ffn_w_gate"], np.float32)[:, perm_cols]
    wu_p = np.asarray(inputs["L1_ffn_w_up"], np.float32)[:, perm_cols]
    wd_p = np.ascontiguousarray(np.asarray(inputs["L1_ffn_w_down"], np.float32)[perm_rows, :])
    def wimg_f(wp):
        out = np.empty((256 * 128, 512), np.float32)
        for g in range(16):
            for dc in range(16):
                out[(g * 16 + dc) * 128:(g * 16 + dc + 1) * 128, :] = \
                    wp[dc * 128:(dc + 1) * 128, g * 512:(g + 1) * 512]
        return np.ascontiguousarray(out)
    wg_i = wimg_f(wg_p)
    wu_i = wimg_f(wu_p)
    prep = {
        "x": x, "wkvv": wkvv, "bkvv": bkvv, "identb": identb, "ident32": ident32,
        "w1T": w1T, "w2T": w2T, "w3T": w3T,
        "qw": _img_lhsT(qwT, 16, np.float32), "kw": _img_lhsT(kwT, 4, np.float32),
        "vw": vwT, "cw": cwT, "cosf": cosf, "sinf": sinf,
        "wg_i": wg_i, "wu_i": wu_i, "wd_p": wd_p,
    }
    return prep

def _core_mask(c):
    kq = c % 4
    mask = np.zeros((128, 512), np.float32)
    ii = np.arange(128)
    for tq in range(4):
        mm = mask[:, tq * 128:(tq + 1) * 128]
        if tq == kq:
            mm[ii[:, None] > ii[None, :]] = -1e30
        elif tq > kq:
            mm[:] = -1e30
    return mask

_BUILD_CACHE = {}

def _get_nc(name, builder):
    nc = bass.Bass("TRN2", target_bir_lowering=False)
    builder(nc)
    return nc

def _run(name, builder, maps):
    nc = _get_nc(name, builder)
    return run_bass_kernel_spmd(nc, maps, core_ids=list(range(NCORES))).results

def kernel(**inputs):
    P = _prep_all(inputs)
    mapsA = [{"x_img": img_tokens(P["x"], c), "wkvv": P["wkvv"], "bkvv": P["bkvv"],
              "identb": P["identb"], "ident32": P["ident32"],
              "w1s": wimg(P["w1T"][c * 256:(c + 1) * 256]),
              "w2s": wimg(P["w2T"][c * 256:(c + 1) * 256]),
              "w3s": wimg(P["w3T"][c * 704:(c + 1) * 704])} for c in range(NCORES)]
    rA = _run("A", build_A, mapsA)
    pall = np.zeros((1, 24), np.float32)
    for c in range(NCORES):
        for m in range(3):
            pall[0, 8 * m + c] = rA[c]["pabs"][0, m]
    mapsB = [{"w1s": mapsA[c]["w1s"], "w2s": mapsA[c]["w2s"], "w3s": mapsA[c]["w3s"],
              "pall": pall} for c in range(NCORES)]
    rB = _run("B", build_B, mapsB)
    w1q = np.empty((D, H0), BF)
    w2q = np.empty((D, H0), BF)
    w3q = np.empty((H0, D), BF)
    for c in range(NCORES):
        for a in range(2):
            w1q[c * 256 + a * 128:c * 256 + (a + 1) * 128] = rB[c]["q1"][:, a * H0:(a + 1) * H0]
            w2q[c * 256 + a * 128:c * 256 + (a + 1) * 128] = rB[c]["q2"][:, a * H0:(a + 1) * H0]
        got = rB[c]["q3"]
        for a in range(6):
            r0 = c * 704 + a * 128
            n = min(128, c * 704 + 704 - r0)
            if n > 0:
                w3q[r0:r0 + n] = got[:n, a * D:(a + 1) * D]
    mapsC = [{"x_img": None, "w1q": _img_lhsT(w1q, 44, BF), "w2q": _img_lhsT(w2q, 44, BF),
              "w3q": w3q, "identb": P["identb"]} for c in range(NCORES)]
    x1 = np.empty((2 * T, D), np.float32)
    for c in range(NCORES):
        x1[core_rows(c)] = unimg_tokens(rA[c]["x1"], D)
    for c in range(NCORES):
        mapsC[c]["x_img"] = img_tokens(x1, c)
    rC = _run("C", build_C, mapsC)
    x2 = np.empty((2 * T, D), np.float32)
    for c in range(NCORES):
        x2[core_rows(c)] = unimg_tokens(rC[c]["x2"], D)
    mapsD = []
    for c in range(NCORES):
        rows = core_rows(c) % T
        mapsD.append({"x_img": img_tokens(x2, c), "qw": P["qw"], "kw": P["kw"], "vw": P["vw"],
                      "cosd": np.ascontiguousarray(P["cosf"][rows].T),
                      "sind": np.ascontiguousarray(P["sinf"][rows].T),
                      "identb": P["identb"]})
    rD = _run("D", build_D, mapsD)
    k_seq = [np.empty((128, 4 * T), BF) for _ in range(2)]
    v_seq = [np.empty((T, 512), BF) for _ in range(2)]
    for c in range(NCORES):
        s = c // 4
        posn = core_rows(c) % T
        kr = rD[c]["k_rot"]
        for g in range(4):
            k_seq[s][:, g * T + posn] = kr[:, g * 1024:(g + 1) * 1024]
        v_seq[s][posn] = unimg_tokens_v(rD[c]["v_t"])
    mapsE = [{"x_img": mapsD[c]["x_img"], "q_rot": rD[c]["q_rot"],
              "k_all": k_seq[c // 4], "v_all": v_seq[c // 4], "cw": P["cw"],
              "mask": _core_mask(c), "identb": P["identb"], "ident32": P["ident32"]}
             for c in range(NCORES)]
    rE = _run("E", build_E, mapsE)
    x3 = np.empty((2 * T, D), np.float32)
    for c in range(NCORES):
        x3[core_rows(c)] = unimg_tokens(rE[c]["x3"], D)
    mapsF = [{"x_img": img_tokens(x3, c), "wg": P["wg_i"], "wu": P["wu_i"],
              "wd": P["wd_p"], "identb": P["identb"]} for c in range(NCORES)]
    rF = _run("F", build_F, mapsF)
    out = np.empty((2 * T, D), np.float32)
    for c in range(NCORES):
        out[core_rows(c)] = unimg_tokens(rF[c]["xo"], D)
    return out.reshape(2, T, D)

def unimg_tokens_v(vt):
    out = np.empty((NT * 128, 512), vt.dtype)
    for t in range(NT):
        out[t * 128:(t + 1) * 128] = vt[:, t * 512:(t + 1) * 512]
    return out
